# version 28 (final)
# baseline (speedup 1.0000x reference)
import numpy as np
import concourse.bass as bass
import concourse.mybir as mybir
from concourse.bass_utils import run_bass_kernel_spmd
from contextlib import ExitStack

F32 = mybir.dt.float32
BF16 = mybir.dt.bfloat16
U32 = mybir.dt.uint32
AF = mybir.ActivationFunctionType
ALU = mybir.AluOpType
AX = mybir.AxisListType

D = 2048
NH = 16
DH = 128
NI = 16
NJ = 4
JW = 512
NT_G = 8
EPS = 1e-5
INV_D = 1.0 / D

ENGINES = ("sync", "gpsimd", "tensor", "vector", "scalar")


class Sched:
    def __init__(self):
        self.stream = {e: [] for e in ENGINES}
        self.cnt = {}

    def inc(self, sem, by):
        self.cnt[sem] = self.cnt.get(sem, 0) + by
        return self.cnt[sem]

    def val(self, sem):
        return self.cnt.get(sem, 0)

    def add(self, eng, fn):
        self.stream[eng].append(fn)


def build_nc(phases=7):
    nc = bass.Bass("TRN2", target_bir_lowering=False)

    d_xT = nc.dram_tensor("xT_own", [D, 1024], F32, kind="ExternalInput").ap()
    d_xo = nc.dram_tensor("x_own", [1024, D], F32, kind="ExternalInput").ap()
    d_Wqk = nc.dram_tensor("Wqk", [D, D], F32, kind="ExternalInput").ap()
    d_Wv = nc.dram_tensor("Wv", [D, D], F32, kind="ExternalInput").ap()
    d_Wu = nc.dram_tensor("Wu", [D, D], F32, kind="ExternalInput").ap()
    d_Wo = nc.dram_tensor("Wo", [D, D], F32, kind="ExternalInput").ap()
    d_WbT = nc.dram_tensor("WbT", [D, D], F32, kind="ExternalInput").ap()
    d_rT = nc.dram_tensor("routerT", [D, 8], F32, kind="ExternalInput").ap()
    d_AT = nc.dram_tensor("AT", [D, 128], F32, kind="ExternalInput").ap()
    d_BwT = nc.dram_tensor("BwT", [128, D], F32, kind="ExternalInput").ap()
    d_bias = nc.dram_tensor("biases", [5, D], F32, kind="ExternalInput").ap()
    d_lng = nc.dram_tensor("ln_g", [D], F32, kind="ExternalInput").ap()
    d_lnb = nc.dram_tensor("ln_b", [D], F32, kind="ExternalInput").ap()
    d_y = nc.dram_tensor("y", [1024, D], F32, kind="ExternalOutput").ap()
    d_cin = nc.dram_tensor("cc_in", [128, 2048], F32, kind="Internal")
    d_cout = nc.dram_tensor("cc_out", [128, 2048], F32, kind="Internal")

    S = Sched()
    with ExitStack() as st:
        E = st.enter_context
        arena1 = E(nc.sbuf_tensor("arena1", [128, 32768], BF16))
        arena2 = E(nc.sbuf_tensor("arena2", [128, 16384], BF16))
        vh = E(nc.sbuf_tensor("vh", [128, 16384], BF16))
        wj = E(nc.sbuf_tensor("wj", [128, 16384], BF16))
        bcst = E(nc.sbuf_tensor("bcst", [128, 2048], F32))
        bb = E(nc.sbuf_tensor("bb", [128, 2048], F32))
        escr2 = E(nc.sbuf_tensor("escr2", [128, 1024], F32))
        scr4k = E(nc.sbuf_tensor("scr4k", [128, 1024], F32))
        rT_bf = E(nc.sbuf_tensor("rT_bf", [128, 16, 8], BF16))
        brow = E(nc.sbuf_tensor("brow", [1, 5 * 2048], BF16))
        ones_bf = E(nc.sbuf_tensor("ones_bf", [1, 128], BF16))
        ident = E(nc.sbuf_tensor("ident", [128, 128], BF16))
        cols = E(nc.sbuf_tensor("cols", [128, 96], F32))
        cvals = E(nc.sbuf_tensor("cvals", [128, 8], F32))
        lgt = E(nc.sbuf_tensor("lgt", [128, 8], F32))
        msk = E(nc.sbuf_tensor("msk", [128, 8], F32))
        bsc = scr4k[:].rearrange("p (a b) -> p a b", b=512)
        AT_bf = scr4k[:].bitcast(BF16).rearrange("p (i e) -> p i e", e=128)
        ps_mm = E(nc.psum_tensor("ps_mm", [128, 4, 512], F32))
        ps_kv = E(nc.psum_tensor("ps_kv", [128, 4, 512], F32))
        SEM = {}
        for name in ("sw", "pe", "dv", "ac", "kvx", "out", "gx",
                     "gc", "gw", "bc", "w0", "w1", "w0b", "w1b", "xt", "xt2",
                     "xt3", "xb", "xb2", "sa", "sb", "cc", "kvy"):
            SEM[name] = E(nc.semaphore("s_" + name))
        block = E(nc.Block())

        def trv(rnd):
            return ps_mm[:, rnd, :].bitcast(BF16)
        a2v = arena2[:].rearrange("p (i n) -> p i n", n=1024)
        wjv = wj[:].rearrange("p (b i c) -> p b i c", b=2, c=JW)
        x1f = arena1[:].bitcast(F32)
        vhf = vh[:].bitcast(F32)
        e_scr = escr2[:, 0:512]
        r_scr = escr2[:, 512:1024]
        BwT_bf = escr2[:].bitcast(BF16)
        o_t = vhf[:, 0:2048]
        kv_mine = vhf[:, 0:2048]
        kv_bf = vh[:, 12288:14336]
        sq_scr = vhf[:, 2048:4096]
        qeT_scr = vh[:, 8192:10240]
        ao_t = vh[:, 10240:12288]
        x1bf = vh[:, 12288:14336]
        h_m = vh[:, 14336:14464]
        h_m2 = vh[:, 15488:15616]
        h_mT = vh[:].rearrange("p (a b) -> p a b", b=128)

        def qe_t(t):
            return arena1[:, t * 2048:(t + 1) * 2048]

        def u_t(t):
            return arena1[:, (8 + t) * 2048:(9 + t) * 2048]

        def vh_t(t):
            return vh[:, t * 2048:(t + 1) * 2048]

        def x1_t(t):
            return x1f[:, t * 2048:(t + 1) * 2048]

        def hmT_t(t):
            return vh[:, 14464 + t * 128:14464 + (t + 1) * 128]

        def dma(eng, out, in_, sem):
            v = S.inc(sem, 16)
            S.add(eng, lambda e, out=out, in_=in_, s=SEM[sem]: e.dma_start(out=out, in_=in_).then_inc(s, 16))
            return v

        HWM = {e: {} for e in ENGINES}

        def wait(eng, sem, v):
            if v > 0 and HWM[eng].get(sem, 0) < v:
                HWM[eng][sem] = v
                S.add(eng, lambda e, s=SEM[sem], v=v: e.wait_ge(s, v))

        CHAIN = {"vector": "dv", "scalar": "ac", "gpsimd": "gx"}

        def op(eng, fn, sem=None):
            ch = CHAIN.get(eng)
            if ch is None:
                if sem is None:
                    S.add(eng, fn)
                    return None
                v = S.inc(sem, 1)
                S.add(eng, lambda e, fn=fn, s=SEM[sem]: fn(e).then_inc(s, 1))
                return v
            prev = S.val(ch)
            wait(eng, ch, prev)
            v = S.inc(ch, 1)
            S.add(eng, lambda e, fn=fn, s=SEM[ch]: fn(e).then_inc(s, 1))
            return v

        def emit_blocks():
            @block.sync
            def _(e):
                for fn in S.stream["sync"]:
                    fn(e)

            @block.gpsimd
            def _(e):
                for fn in S.stream["gpsimd"]:
                    fn(e)

            @block.tensor
            def _(e):
                for fn in S.stream["tensor"]:
                    fn(e)

            @block.vector
            def _(e):
                for fn in S.stream["vector"]:
                    fn(e)

            @block.scalar
            def _(e):
                for fn in S.stream["scalar"]:
                    fn(e)

        xT3 = d_xT.rearrange("(i p) n -> p i n", p=128)
        src0 = d_Wqk[:, 0:JW].rearrange("(i p) c -> p i c", p=128)
        w0_h1 = dma("gpsimd", wjv[:, 0, 0:8, :], src0[:, 0:8, :], "w0")
        xT_d1a = dma("gpsimd", a2v[:, 0:8, 0:512], xT3[:, 0:8, 0:512], "xt")
        w0_h2 = dma("gpsimd", wjv[:, 0, 8:16, :], src0[:, 8:16, :], "w0b")
        xT_d1b = dma("gpsimd", a2v[:, 8:16, 0:512], xT3[:, 8:16, 0:512], "xt2")
        xT_done = dma("gpsimd", a2v[:, :, 512:1024], xT3[:, :, 512:1024], "xt3")
        CONSTS = [EPS, -0.5, INV_D, -1.0]
        for ci, cv in enumerate(CONSTS):
            op("gpsimd", lambda e, ci=ci, cv=cv: e.memset(cvals[:, ci:ci + 1], cv), "gx")
            nc.const_aps.register(F32, cv, cvals[:, ci:ci + 1])
        consts_gx = S.val("gx")
        v_ms = op("gpsimd", lambda e: e.memset(ident[:], 0.0), "gx")
        wait("gpsimd", "gx", v_ms)
        v_id = op("gpsimd", lambda e: e.affine_select(out=ident[:], in_=ident[:],
                                                      compare_op=ALU.not_equal, fill=1.0, base=0,
                                                      pattern=[[-1, 128]], channel_multiplier=1), "gx")
        v_on = op("gpsimd", lambda e: e.memset(ones_bf[:], 1.0), "gx")
        ident_ready = v_id
        ones_ready = v_on
        for w5 in range(5):
            c_brow = dma("gpsimd", brow[0:1, w5 * 2048:(w5 + 1) * 2048],
                         d_bias[w5:w5 + 1, :], "gc")
        c_bb = dma("sync", bb[:], d_lnb[None, :].partition_broadcast(128), "sw")
        c_consts = S.val("gc")

        gc_all = [S.val("gc")]
        wload_count = [0]
        wj_free = [0, 0]

        def load_wchunk(dW, j, eng="gpsimd"):
            par = wload_count[0] % 2
            wload_count[0] += 1
            wait(eng, "pe", wj_free[par])
            src = dW[:, j * JW:(j + 1) * JW].rearrange("(i p) c -> p i c", p=128)
            v1 = dma(eng, wjv[:, par, 0:8, :], src[:, 0:8, :], "w%d" % par)
            v2 = dma(eng, wjv[:, par, 8:16, :], src[:, 8:16, :], "w%db" % par)
            return par, (v1, v2)

        bank_free = [0, 0, 0, 0]
        bank_free_sem = ["dv", "dv", "dv", "dv"]
        grp_counter = [0]

        def mm_group(lhsT_blocks, wpar, bias_idx, j, extra_waits, extra_mms=None,
                     mid_waits=None):
            g = grp_counter[0]
            grp_counter[0] += 1
            bank = g % 4
            wait("tensor", bank_free_sem[bank], bank_free[bank])
            for sem, v in extra_waits:
                wait("tensor", sem, v)
            tgt = ps_mm[:, bank, 0:JW]
            for i in range(16):
                if i == 8 and mid_waits:
                    for sem, v in mid_waits:
                        wait("tensor", sem, v)
                lhsT = lhsT_blocks(i)
                rhs = wjv[:, wpar, i, :]
                if i == 15 and bias_idx is None and not extra_mms:
                    v = op("tensor", lambda e, t=tgt, l=lhsT, r=rhs: e.matmul(
                        t, l, r, start=False, stop=True), "pe")
                    return bank, v
                S.add("tensor", lambda e, t=tgt, l=lhsT, r=rhs, s=(i == 0): e.matmul(t, l, r, start=s, stop=False))
            if extra_mms:
                for l, r in extra_mms:
                    S.add("tensor", lambda e, t=tgt, l=l, r=r: e.matmul(t, l, r, start=False, stop=False))
            if bias_idx is None:
                v = op("tensor", lambda e, t=tgt: e.matmul(
                    t, ones_bf[:], brow[0:1, 0:JW], start=False, stop=True,
                    skip_group_check=True), "pe")
                return bank, v
            wait("tensor", "gc", gc_all[0])
            bias_rhs = brow[0:1, bias_idx * 2048 + j * JW: bias_idx * 2048 + (j + 1) * JW]
            v = op("tensor", lambda e, t=tgt, b=bias_rhs: e.matmul(t, ones_bf[:], b, start=False, stop=True), "pe")
            return bank, v

        arena2_user = 0
        qe_done = {}
        vh_done = {}
        par_free = [0, 0]

        wait("tensor", "gx", ones_ready)
        pre = (0, (w0_h1, w0_h2))
        wload_count[0] = 1
        badd_par = [0]
        scr_free = [0, 0]

        def bias_add(bank, pev, c_b, j):
            par2 = badd_par[0] % 2
            badd_par[0] += 1
            wait("vector", "pe", pev)
            wait("vector", "bc", c_b)
            wait("vector", "ac", scr_free[par2])
            dv_b = op("vector", lambda e, b=bank, par2=par2, j=j: e.scalar_tensor_tensor(
                out=bsc[:, par2, :], in0=ps_mm[:, b, 0:JW], scalar=0.0,
                in1=bcst[:, j * JW:(j + 1) * JW], op0=ALU.add, op1=ALU.add), "dv")
            bank_free[bank] = dv_b
            bank_free_sem[bank] = "dv"
            return par2, dv_b

        c_bq = dma("sync", bcst[:], d_bias[0:1, :].partition_broadcast(128), "bc")
        for j in range(NJ):
            par, vw = pre if j == 0 else load_wchunk(d_Wqk, j)
            for tl in range(NT_G):
                t = tl
                ew = [("w%d" % par, vw[0]), ("xt", xT_d1a)]
                mid = [("w%db" % par, vw[1]), ("xt2", xT_d1b)]
                if tl >= 1:
                    ew.append(("xt2", xT_d1b))
                if tl >= 4:
                    ew.append(("xt3", xT_done))
                bank, pev = mm_group(
                    lambda i, tl=tl: a2v[:, i, tl * 128:(tl + 1) * 128],
                    par, None, j, ew, mid_waits=mid)
                arena2_user = pev
                par2, dv_b = bias_add(bank, pev, c_bq, j)
                wait("scalar", "dv", dv_b)
                op("scalar", lambda e, par2=par2: e.activation(
                    out=e_scr, in_=bsc[:, par2, :], func=AF.Exp), "ac")
                ac_r = op("scalar", lambda e, par2=par2: e.activation(
                    out=r_scr, in_=bsc[:, par2, :], func=AF.Relu), "ac")
                scr_free[par2] = ac_r
                wait("vector", "ac", ac_r)
                dv_q = op("vector", lambda e, t=t, j=j: e.scalar_tensor_tensor(
                    out=qe_t(t)[:, j * JW:(j + 1) * JW], in0=e_scr,
                    scalar=-1.0, in1=r_scr, op0=ALU.add, op1=ALU.min), "dv")
                par_free[0] = dv_q
                qe_done[t] = dv_q
            wj_free[par] = pev
        wait("sync", "dv", S.val("dv"))
        c_bv = dma("sync", bcst[:], d_bias[1:2, :].partition_broadcast(128), "bc")
        for j in range(NJ):
            par, vw = load_wchunk(d_Wv, j)
            for tl in range(NT_G):
                ew = [("w%d" % par, vw[0]), ("xt", xT_d1a), ("xt2", xT_d1b)]
                if tl >= 4:
                    ew.append(("xt3", xT_done))
                bank, pev = mm_group(
                    lambda i, tl=tl: a2v[:, i, tl * 128:(tl + 1) * 128],
                    par, None, j, ew, mid_waits=[("w%db" % par, vw[1])])
                arena2_user = pev
                par2, dv_b = bias_add(bank, pev, c_bv, j)
                wait("scalar", "dv", dv_b)
                ac_v = op("scalar", lambda e, par2=par2, tl=tl, j=j: e.activation(
                    out=vh_t(tl)[:, j * JW:(j + 1) * JW], in_=bsc[:, par2, :],
                    func=AF.Silu), "ac")
                scr_free[par2] = ac_v
                vh_done[tl] = ac_v
            wj_free[par] = pev
        for tl in range(NT_G):
            t = tl
            wait("tensor", "ac", vh_done[tl])
            wait("tensor", "dv", qe_done[t])
            first = (tl == 0)
            last = (tl == NT_G - 1)
            for h in range(NH):
                tgt = ps_kv[:, h // 4, (h % 4) * 128:(h % 4 + 1) * 128]
                st = first and (h % 4 == 0)
                sp = last and (h % 4 == 3)
                fn = (lambda e, tgt=tgt, l=qe_t(t)[:, h * 128:(h + 1) * 128],
                      r=vh_t(tl)[:, h * 128:(h + 1) * 128], st=st, sp=sp:
                      e.matmul(tgt, l, r, start=st, stop=sp, skip_group_check=True))
                if last and h == NH - 1:
                    kv_all_pe = op("tensor", fn, "pe")
                else:
                    S.add("tensor", fn)

        wait("vector", "pe", kv_all_pe)
        for b4 in range(4):
            kv_evac = op("vector", lambda e, b=b4: e.tensor_copy(
                kv_mine[:, b * 512:(b + 1) * 512], ps_kv[:, b, :]), "dv")
        kv_evac_dv = kv_evac
        wait("gpsimd", "dv", par_free[0])
        c_BwT = dma("gpsimd", BwT_bf, d_BwT, "gc")
        gc_all[0] = S.val("gc")
        wu_pre = [load_wchunk(d_Wu, 0), load_wchunk(d_Wu, 1)]
        wait("gpsimd", "dv", kv_evac_dv)
        kv_out_v = dma("gpsimd", d_cin.ap(), kv_mine, "kvx")
        wait("gpsimd", "kvx", kv_out_v)
        S.add("gpsimd", lambda e: e.collective_compute(
            "AllReduce", ALU.add,
            replica_groups=[[0, 1, 2, 3], [4, 5, 6, 7]],
            ins=[d_cin.ap().opt()], outs=[d_cout.ap().opt()]).then_inc(SEM["cc"]))

        def finish_early():
            wait("sync", "sw", S.val("sw"))
            for t in range(NT_G):
                dma("sync", d_y[t * 128:(t + 1) * 128, :], gb[:], "out")
            wait("sync", "out", S.val("out"))

        if phases < 2:
            finish_early()
            emit_blocks()
            return nc

        if phases < 3:
            finish_early()
            emit_blocks()
            return nc

        wait("sync", "dv", S.val("dv"))
        c_bu = dma("sync", bcst[:], d_bias[2:3, :].partition_broadcast(128), "bc")
        wait("sync", "cc", 1)
        kv_in_v = dma("sync", kv_mine, d_cout.ap(), "kvy")
        for j in range(NJ):
            if j < 2:
                par, vw = wu_pre[j]
            else:
                par, vw = load_wchunk(d_Wu, j)
            for tl in range(NT_G):
                ew = [("w%d" % par, vw[0]), ("xt", xT_d1a), ("xt2", xT_d1b), ("xt3", xT_done)]
                bank, pev = mm_group(
                    lambda i, tl=tl: a2v[:, i, tl * 128:(tl + 1) * 128],
                    par, None, j, ew, mid_waits=[("w%db" % par, vw[1])])
                arena2_user = pev
                par2, dv_b = bias_add(bank, pev, c_bu, j)
                wait("scalar", "dv", dv_b)
                ac_u = op("scalar", lambda e, par2=par2, tl=tl, j=j: e.activation(
                    out=u_t(tl)[:, j * JW:(j + 1) * JW], in_=bsc[:, par2, :],
                    func=AF.Silu), "ac")
                scr_free[par2] = ac_u
            wj_free[par] = pev
        u_done_ac = S.val("ac")
        wait("sync", "dv", S.val("dv"))
        c_gb2 = dma("sync", bcst[:], d_lng[None, :].partition_broadcast(128), "bc")
        wait("vector", "kvy", kv_in_v)
        kv_ready = op("vector", lambda e: e.tensor_copy(kv_bf, kv_mine), "dv")

        if phases < 4:
            finish_early()
            emit_blocks()
            return nc

        x1v = x1f[:].rearrange("p (s n) -> p s n", n=1024)

        def x1_2b(t):
            return x1v[:, t:t + 9:8, :]

        def x1_half(t, h):
            return x1v[:, t + 8 * h, :]

        sq2b = vhf[:, 0:2048].rearrange("p (s n) -> p s n", n=1024)

        out_free = [(None, 0)] * 4
        aoT_ready = {}
        norm_done = {}
        gate_done = {}
        qeT_evac = {}
        trA_pe = {}
        aotr_pe = {}

        def o_buf(t):
            return vhf[:, 0:2048] if t % 2 == 0 else vhf[:, 2048:4096]

        def o_sq(t):
            return vhf[:, 2048:4096] if t % 2 == 0 else vhf[:, 0:2048]

        def ln_chain(acc_ap, src, sqt, tgt):
            op("scalar", lambda e, a=acc_ap: e.activation(out=cols[:, 80:84], in_=a,
                                                          func=AF.Copy, accum_out=cols[:, 4:5]), "ac")
            op("scalar", lambda e, s=src, q=sqt: e.activation(
                out=q, in_=s, func=AF.Square, accum_out=cols[:, 5:6]), "ac")
            op("scalar", lambda e: e.mul(cols[:, 6:7], cols[:, 4:5], -INV_D), "ac")
            op("scalar", lambda e: e.mul(cols[:, 7:8], cols[:, 5:6], INV_D), "ac")
            ac_mu2 = op("scalar", lambda e: e.activation(out=cols[:, 8:9], in_=cols[:, 6:7],
                                                         func=AF.Square), "ac")
            wait("gpsimd", "ac", ac_mu2)
            gx_var = op("gpsimd", lambda e: e.tensor_sub(
                cols[:, 9:10], cols[:, 7:8], cols[:, 8:9]), "gx")
            wait("scalar", "gx", gx_var)
            op("scalar", lambda e: e.activation(out=cols[:, 10:11], in_=cols[:, 9:10],
                                                func=AF.Ln, bias=EPS), "ac")
            ac_rstd = op("scalar", lambda e: e.activation(out=cols[:, 11:12], in_=cols[:, 10:11],
                                                          func=AF.Exp, scale=-0.5), "ac")
            wait("gpsimd", "ac", ac_rstd)
            gx_nmr = op("gpsimd", lambda e: e.tensor_mul(
                cols[:, 12:13], cols[:, 6:7], cols[:, 11:12]), "gx")
            wait("scalar", "gx", gx_nmr)
            ac_n = op("scalar", lambda e, g=tgt: e.activation(out=g, in_=g, func=AF.Identity,
                                                              scale=cols[:, 11:12], bias=cols[:, 12:13]), "ac")
            return ac_n

        def p3a_A(t):
            for rnd in range(2):
                wait("tensor", bank_free_sem[rnd], bank_free[rnd])
                for hh in range(8):
                    h = rnd * 8 + hh
                    fn = (lambda e, h=h, hh=hh, rnd=rnd, t=t: e.transpose(
                        trv(rnd)[:, hh * 128:(hh + 1) * 128],
                        qe_t(t)[:, h * 128:(h + 1) * 128], ident[:]))
                    if hh == 7:
                        pe_tr = op("tensor", fn, "pe")
                    else:
                        S.add("tensor", fn)
                wait("vector", "pe", pe_tr)
                dv_tr = op("vector", lambda e, rnd=rnd: e.tensor_copy(
                    qeT_scr[:, rnd * 1024:(rnd + 1) * 1024], trv(rnd)), "dv")
                bank_free[rnd] = dv_tr
                bank_free_sem[rnd] = "dv"
            trA_pe[t] = pe_tr
            qeT_evac[t] = dv_tr

        def p3a_front(t):
            ob = o_buf(t)
            wait("tensor", "dv", max(qeT_evac[t], kv_ready, kv_evac_dv))
            for sem, v in out_free:
                if sem:
                    wait("tensor", sem, v)
            for h in range(NH):
                fn = (lambda e, h=h: e.matmul(
                    ps_kv[:, h // 4, (h % 4) * 128:(h % 4 + 1) * 128],
                    qeT_scr[:, h * 128:(h + 1) * 128],
                    kv_bf[:, h * 128:(h + 1) * 128],
                    start=True, stop=True, skip_group_check=True))
                if h == NH - 1:
                    pe_out = op("tensor", fn, "pe")
                else:
                    S.add("tensor", fn)
            wait("scalar", "pe", pe_out)
            if t == 0:
                wait("scalar", "dv", kv_ready)
                wait("scalar", "gx", consts_gx)
            if t >= 2:
                wait("scalar", "dv", gate_done[t - 2])
            for b4 in range(4):
                ac_o = op("scalar", lambda e, b=b4, ob=ob: e.activation(
                    out=ob[:, b * 512:(b + 1) * 512], in_=ps_kv[:, b, :], func=AF.Copy,
                    accum_out=cols[:, b:b + 1]), "ac")
                out_free[b4] = ("ac", ac_o)
            if t >= 1:
                wait("scalar", "dv", gate_done[t - 1])
            norm_done[t] = ln_chain(cols[:, 0:4], ob, o_sq(t), ob)

        ao2 = vh[:, 14336:16384]

        def ao_buf(t):
            return ao_t if t % 2 == 0 else ao2

        def p3a_gate(t):
            ob = o_buf(t)
            obA, obB = ob[:, 0:1024], ob[:, 1024:2048]
            aoA = ao_buf(t)[:, 0:1024]
            aoB = ao_buf(t)[:, 1024:2048]
            uA = u_t(t)[:, 0:1024]
            uB = u_t(t)[:, 1024:2048]
            wait("gpsimd", "ac", norm_done[t])
            if t == 0:
                wait("gpsimd", "sw", c_bb)
                wait("gpsimd", "bc", c_gb2)
            if t >= 2:
                wait("gpsimd", "pe", aotr_pe[t - 2])
            op("gpsimd", lambda e, obB=obB: e.tensor_mul(obB, obB, bcst[:, 1024:2048]), "gx")
            gx_b = op("gpsimd", lambda e, obB=obB: e.tensor_add(obB, obB, bb[:, 1024:2048]), "gx")
            wait("vector", "ac", norm_done[t])
            if t == 0:
                wait("vector", "sw", c_bb)
                wait("vector", "bc", c_gb2)
            if t >= 2:
                wait("vector", "pe", aotr_pe[t - 2])
            op("vector", lambda e, obA=obA: e.tensor_mul(obA, obA, bcst[:, 0:1024]))
            op("vector", lambda e, obA=obA: e.tensor_add(obA, obA, bb[:, 0:1024]))
            op("vector", lambda e, aoA=aoA, obA=obA, uA=uA: e.tensor_tensor(aoA, obA, uA, ALU.mult))
            wait("vector", "gx", gx_b)
            dv_ao = op("vector", lambda e, aoB=aoB, obB=obB, uB=uB: e.tensor_tensor(aoB, obB, uB, ALU.mult), "dv")
            gate_done[t] = dv_ao

        def p3a_aotr(t):
            dv_ao = gate_done[t]
            for rnd in range(2):
                bk = 2 + rnd
                wait("tensor", bank_free_sem[bk], bank_free[bk])
                wait("tensor", "dv", dv_ao)
                if t == 0 and rnd == 0:
                    wait("tensor", "pe", arena2_user)
                for hh in range(8):
                    i = rnd * 8 + hh
                    fn = (lambda e, i=i, hh=hh, bk=bk, t=t: e.transpose(
                        trv(bk)[:, hh * 128:(hh + 1) * 128],
                        ao_buf(t)[:, i * 128:(i + 1) * 128], ident[:]))
                    if hh == 7:
                        pe_tr = op("tensor", fn, "pe")
                    else:
                        S.add("tensor", fn)
                wait("vector", "pe", pe_tr)
                dv_a = op("vector", lambda e, rnd=rnd, bk=bk, t=t: e.tensor_copy(
                    a2v[:, rnd * 8:(rnd + 1) * 8, t * 128:(t + 1) * 128],
                    trv(bk)[:].rearrange("p (h n) -> p h n", n=128)), "dv")
                bank_free[bk] = dv_a
                bank_free_sem[bk] = "dv"
                aotr_pe[t] = pe_tr
            aoT_ready[t] = dv_a

        wo_ch = {jj: load_wchunk(d_Wo, jj) for jj in (0, 1)}
        wait("gpsimd", "ac", u_done_ac)
        c_AT = dma("gpsimd", AT_bf[:, :, :], d_AT.rearrange("(i p) e -> p i e", p=128), "gw")
        c_rT = dma("gpsimd", rT_bf[:], d_rT.rearrange("(i p) e -> p i e", p=128), "gw")
        wo_last_pev = {}
        xoA_done = {}
        xoB_done = {}

        def loopA_backA(s):
            p3a_aotr(s)
            wait("sync", "pe", trA_pe[s])
            wait("sync", "sa", S.val("sa"))
            xoA_done[s] = dma("sync", x1_half(s, 0), d_xo[s * 128:(s + 1) * 128, 0:1024], "sa")
            wait("sync", "dv", gate_done[s])
            wait("sync", "sb", S.val("sb"))
            xoB_done[s] = dma("sync", x1_half(s, 1), d_xo[s * 128:(s + 1) * 128, 1024:2048], "sb")

        def loopA_backW(s):
            for jj in (0, 1):
                par, vw = wo_ch[jj]
                ew = [("w%d" % par, vw[0]), ("dv", aoT_ready[s])]
                bank, pev = mm_group(
                    lambda i, s=s: a2v[:, i, s * 128:(s + 1) * 128], par, 3, jj, ew,
                    mid_waits=[("w%db" % par, vw[1])])
                wo_last_pev[jj] = (par, pev)
                wait("vector", "pe", pev)
                wait("vector", "sa", xoA_done[s])
                dv_r = op("vector", lambda e, b=bank, s=s, jj=jj: e.scalar_tensor_tensor(
                    out=x1_half(s, 0)[:, jj * JW:(jj + 1) * JW], in0=ps_mm[:, b, 0:JW], scalar=0.0,
                    in1=x1_half(s, 0)[:, jj * JW:(jj + 1) * JW], op0=ALU.add, op1=ALU.add,
                    accum_out=cols[:, 16 + s * NJ + jj:17 + s * NJ + jj]), "dv")
                bank_free[bank] = dv_r
                bank_free_sem[bank] = "dv"

        p3a_A(0)
        for t in range(NT_G):
            p3a_front(t)
            if t + 1 < NT_G:
                p3a_A(t + 1)
            if t >= 1:
                loopA_backA(t - 1)
            if t >= 2:
                loopA_backW(t - 2)
            p3a_gate(t)
        loopA_backA(NT_G - 1)
        loopA_backW(NT_G - 2)
        loopA_backW(NT_G - 1)
        for jj in (0, 1):
            par, pev = wo_last_pev[jj]
            wj_free[par] = pev
        p3a_last_dv = S.val("dv")
        p3a_last_ac = S.val("ac")

        if phases < 5:
            finish_early()
            emit_blocks()
            return nc

        wo_ch2 = {jj: load_wchunk(d_Wo, jj) for jj in (2, 3)}
        ln2_ac = {}
        x1T_ready = {}
        hmT_ready = {}
        cast_xb = {}
        x1bf_user = [0, 0]
        x1bf2 = qeT_scr

        def x1bf_buf(t):
            return x1bf if t % 2 == 0 else x1bf2

        def loopB_front(t):
            for jj in (2, 3):
                par, vw = wo_ch2[jj]
                ew = [("w%d" % par, vw[0]), ("dv", aoT_ready[t])]
                bank, pev = mm_group(
                    lambda i, t=t: a2v[:, i, t * 128:(t + 1) * 128], par, 3, jj, ew,
                    mid_waits=[("w%db" % par, vw[1])])
                wo_last_pev[jj] = (par, pev)
                wait("vector", "pe", pev)
                wait("vector", "sb", xoB_done[t])
                dv_r = op("vector", lambda e, b=bank, t=t, jj=jj: e.scalar_tensor_tensor(
                    out=x1_half(t, 1)[:, (jj - 2) * JW:(jj - 1) * JW], in0=ps_mm[:, b, 0:JW], scalar=0.0,
                    in1=x1_half(t, 1)[:, (jj - 2) * JW:(jj - 1) * JW], op0=ALU.add, op1=ALU.add,
                    accum_out=cols[:, 16 + t * NJ + jj:17 + t * NJ + jj]), "dv")
                bank_free[bank] = dv_r
                bank_free_sem[bank] = "dv"
            wait("scalar", "dv", dv_r)
            ln2_ac[t] = ln_chain(cols[:, 16 + t * NJ:16 + t * NJ + NJ], x1_2b(t), sq2b, x1_2b(t))
        def loopB_cast(t):
            wait("gpsimd", "ac", ln2_ac[t])
            wait("gpsimd", "pe", x1bf_user[t % 2])
            cast_xb[t] = ("xb" if t % 2 == 0 else "xb2",
                          dma("gpsimd", x1bf_buf(t), x1_2b(t), "xb" if t % 2 == 0 else "xb2"))

        dvh_t = {}

        def hm_buf(t):
            return h_m if t % 2 == 0 else h_m2

        def loopB_hmT(s):
            wait("tensor", bank_free_sem[0], bank_free[0])
            wait("tensor", "dv", dvh_t[s])
            pe_ht = op("tensor", lambda e, s=s: e.transpose(trv(0)[:, 0:128], hm_buf(s)[:], ident[:]), "pe")
            wait("vector", "pe", pe_ht)
            dv_ht = op("vector", lambda e, s=s: e.tensor_copy(hmT_t(s), trv(0)[:, 0:128]), "dv")
            bank_free[0] = dv_ht
            bank_free_sem[0] = "dv"
            hmT_ready[s] = dv_ht

        def loopB_tail(t):
            if t >= 1:
                loopB_hmT(t - 1)
            xbb = x1bf_buf(t)
            for rnd in range(2):
                bk = 2 + rnd
                wait("tensor", cast_xb[t][0], cast_xb[t][1])
                wait("tensor", bank_free_sem[bk], bank_free[bk])
                for hh in range(8):
                    i = rnd * 8 + hh
                    fn = (lambda e, i=i, hh=hh, bk=bk, xbb=xbb: e.transpose(
                        trv(bk)[:, hh * 128:(hh + 1) * 128],
                        xbb[:, i * 128:(i + 1) * 128], ident[:]))
                    if hh == 7:
                        pe_tr = op("tensor", fn, "pe")
                    else:
                        S.add("tensor", fn)
                x1bf_user[t % 2] = pe_tr
                wait("vector", "pe", pe_tr)
                dv_x = op("vector", lambda e, rnd=rnd, bk=bk, t=t: e.tensor_copy(
                    a2v[:, rnd * 8:(rnd + 1) * 8, t * 128:(t + 1) * 128],
                    trv(bk)[:].rearrange("p (h n) -> p h n", n=128)), "dv")
                bank_free[bk] = dv_x
                bank_free_sem[bk] = "dv"
            x1T_ready[t] = dv_x
            wait("tensor", "gw", c_rT)
            wait("tensor", "dv", x1T_ready[t])
            if out_free[0][0]:
                wait("tensor", out_free[0][0], out_free[0][1])
            for i in range(16):
                fn = (lambda e, i=i, t=t: e.matmul(
                    ps_kv[:, 0, 0:8], a2v[:, i, t * 128:(t + 1) * 128],
                    rT_bf[:, i, :], start=(i == 0), stop=(i == 15), skip_group_check=True))
                if i == 15:
                    pe_r = op("tensor", fn, "pe")
                else:
                    S.add("tensor", fn)
            wait("tensor", "gw", c_rT)
            if out_free[1][0]:
                wait("tensor", out_free[1][0], out_free[1][1])
            for i in range(16):
                fn = (lambda e, i=i, t=t: e.matmul(
                    ps_kv[:, 1, 0:128], a2v[:, i, t * 128:(t + 1) * 128],
                    AT_bf[:, i, :], start=(i == 0), stop=(i == 15), skip_group_check=True))
                if i == 15:
                    pe_h = op("tensor", fn, "pe")
                else:
                    S.add("tensor", fn)
            wait("vector", "pe", pe_r)
            op("vector", lambda e: e.tensor_copy(lgt[:], ps_kv[:, 0, 0:8]))
            dv_l = op("vector", lambda e: e.reduce_max(cols[:, 88:89], lgt[:], axis=AX.X), "dv")
            out_free[0] = ("dv", dv_l)
            op("vector", lambda e: e.tensor_scalar(out=msk[:], in0=lgt[:],
                                                   scalar1=cols[:, 88:89], scalar2=None, op0=ALU.is_ge))
            op("vector", lambda e: e.tensor_scalar(out=msk[:], in0=msk[:],
                                                   scalar1=2.0, scalar2=None, op0=ALU.mult))
            wait("vector", "pe", pe_h)
            for ex in range(8):
                dv_h = op("vector", lambda e, ex=ex, t=t: e.tensor_scalar(
                    out=hm_buf(t)[:, ex * 16:(ex + 1) * 16], in0=ps_kv[:, 1, ex * 16:(ex + 1) * 16],
                    scalar1=msk[:, ex:ex + 1], scalar2=None, op0=ALU.mult), "dv")
            out_free[1] = ("dv", dv_h)
            dvh_t[t] = dv_h

        for t in range(NT_G):
            loopB_front(t)
            if t >= 2:
                loopB_tail(t - 2)
            loopB_cast(t)
        loopB_tail(NT_G - 2)
        loopB_tail(NT_G - 1)
        loopB_hmT(NT_G - 1)
        for jj in (2, 3):
            par, pev = wo_last_pev[jj]
            wj_free[par] = pev

        if phases < 6:
            finish_early()
            emit_blocks()
            return nc

        wb_ch = {jj: load_wchunk(d_WbT, jj) for jj in (0, 1)}
        wb_last_pev = {}
        for t in range(NT_G):
            for jj in (0, 1):
                par, vw = wb_ch[jj]
                ew = [("w%d" % par, vw[0]), ("gc", c_BwT), ("dv", max(x1T_ready[t], hmT_ready[t]))]
                bank, pev = mm_group(
                    lambda i, t=t: a2v[:, i, t * 128:(t + 1) * 128], par, 4, jj, ew,
                    mid_waits=[("w%db" % par, vw[1])],
                    extra_mms=[(hmT_t(t), BwT_bf[:, jj * JW:(jj + 1) * JW])])
                wb_last_pev[jj] = (par, pev)
                wait("vector", "pe", pev)
                wait("vector", "ac", ln2_ac[t])
                wait("vector", cast_xb[t][0], cast_xb[t][1])
                dv_r = op("vector", lambda e, b=bank, t=t, jj=jj: e.scalar_tensor_tensor(
                    out=x1_half(t, 0)[:, jj * JW:(jj + 1) * JW], in0=ps_mm[:, b, 0:JW], scalar=0.0,
                    in1=x1_half(t, 0)[:, jj * JW:(jj + 1) * JW], op0=ALU.add, op1=ALU.add,
                    accum_out=cols[:, 16 + t * NJ + jj:17 + t * NJ + jj]), "dv")
                bank_free[bank] = dv_r
                bank_free_sem[bank] = "dv"
        for jj in (0, 1):
            par, pev = wb_last_pev[jj]
            wj_free[par] = pev

        if phases < 7:
            finish_early()
            emit_blocks()
            return nc

        wb_ch2 = {jj: load_wchunk(d_WbT, jj) for jj in (2, 3)}
        for t in range(NT_G):
            for jj in (2, 3):
                par, vw = wb_ch2[jj]
                ew = [("w%d" % par, vw[0]), ("gc", c_BwT), ("dv", max(x1T_ready[t], hmT_ready[t]))]
                bank, pev = mm_group(
                    lambda i, t=t: a2v[:, i, t * 128:(t + 1) * 128], par, 4, jj, ew,
                    mid_waits=[("w%db" % par, vw[1])],
                    extra_mms=[(hmT_t(t), BwT_bf[:, jj * JW:(jj + 1) * JW])])
                wb_last_pev[jj] = (par, pev)
                wait("vector", "pe", pev)
                wait("vector", "ac", ln2_ac[t])
                wait("vector", cast_xb[t][0], cast_xb[t][1])
                dv_r = op("vector", lambda e, b=bank, t=t, jj=jj: e.scalar_tensor_tensor(
                    out=x1_half(t, 1)[:, (jj - 2) * JW:(jj - 1) * JW], in0=ps_mm[:, b, 0:JW], scalar=0.0,
                    in1=x1_half(t, 1)[:, (jj - 2) * JW:(jj - 1) * JW], op0=ALU.add, op1=ALU.add,
                    accum_out=cols[:, 16 + t * NJ + jj:17 + t * NJ + jj]), "dv")
                bank_free[bank] = dv_r
                bank_free_sem[bank] = "dv"
            wait("scalar", "dv", dv_r)
            ac_n = ln_chain(cols[:, 16 + t * NJ:16 + t * NJ + NJ], x1_2b(t), sq2b, x1_2b(t))
            wait("sync", "ac", ac_n)
            dma("sync", d_y[t * 128:(t + 1) * 128, :], x1_2b(t), "out")
        wait("sync", "out", S.val("out"))

        emit_blocks()
    return nc


_NC_CACHE = [None]


def kernel(**inputs):
    x = np.asarray(inputs["L0_attn_x"], dtype=np.float32)
    Wqk = np.ascontiguousarray(inputs["L0_attn_Wqk"], dtype=np.float32)
    Wv = np.ascontiguousarray(inputs["L0_attn_Wv"], dtype=np.float32)
    Wu = np.ascontiguousarray(inputs["L0_attn_Wu"], dtype=np.float32)
    Wo = np.ascontiguousarray(inputs["L0_attn_Wo"], dtype=np.float32)
    WbT = np.ascontiguousarray(np.asarray(inputs["L0_ffn_W_base"], dtype=np.float32).T)
    routerT = np.ascontiguousarray(np.asarray(inputs["L0_ffn_router_W"], dtype=np.float32).T)
    AT = np.ascontiguousarray(np.asarray(inputs["L0_ffn_A"], dtype=np.float32).reshape(128, D).T)
    BwT = np.ascontiguousarray(
        np.asarray(inputs["L0_ffn_Bw"], dtype=np.float32).transpose(0, 2, 1).reshape(128, D))
    biases = np.ascontiguousarray(np.stack([
        np.asarray(inputs["L0_attn_bqk"], dtype=np.float32),
        np.asarray(inputs["L0_attn_bv"], dtype=np.float32),
        np.asarray(inputs["L0_attn_bu"], dtype=np.float32),
        np.asarray(inputs["L0_attn_bo"], dtype=np.float32),
        np.asarray(inputs["L0_ffn_b_base"], dtype=np.float32)]))
    ln_g = np.ascontiguousarray(inputs["L0_attn_ln_g"], dtype=np.float32)
    ln_b = np.ascontiguousarray(inputs["L0_attn_ln_b"], dtype=np.float32)

    in_maps = []
    own_slices = []
    xT_b = [np.ascontiguousarray(x[b].T) for b in range(2)]
    for c in range(8):
        b = c // 4
        q = c % 4
        own = slice(q * 1024, (q + 1) * 1024)
        xT = np.ascontiguousarray(xT_b[b][:, own])
        own_slices.append((b, own))
        x_own = np.ascontiguousarray(x[b, own, :])
        in_maps.append({
            "xT_own": xT, "x_own": x_own,
            "Wqk": Wqk, "Wv": Wv, "Wu": Wu, "Wo": Wo, "WbT": WbT,
            "routerT": routerT, "AT": AT, "BwT": BwT, "biases": biases,
            "ln_g": ln_g, "ln_b": ln_b,
        })

    nc = build_nc()
    res = run_bass_kernel_spmd(nc, in_maps, core_ids=list(range(8)))
    out = np.empty((2, 4096, D), dtype=np.float32)
    for c in range(8):
        b, own = own_slices[c]
        out[b, own, :] = res.results[c]["y"]
    return out
